# version 07 (final)
# baseline (speedup 1.0000x reference)
import numpy as np
import concourse.bass as bass
import concourse.mybir as mybir
from concourse.bass_utils import run_bass_kernel_spmd

AL = mybir.AluOpType
AF = mybir.ActivationFunctionType
AX = mybir.AxisListType
F32 = mybir.dt.float32
U32 = mybir.dt.uint32

B, S, D, F, E = 4, 8192, 1024, 256, 8
NTF = 16
CAP = 640
CGLOB = 4096.0
DF = 4096
EPS = 1e-5
INVD = 1.0 / float(D)
PI = float(np.pi)
TWO_PI = float(2.0 * np.pi)

DEBUG = {"on": False}
STREAMS = ["T", "V", "C", "G", "S", "dS", "dG"]
EMAP = {"T": "tensor", "V": "vector", "C": "scalar", "G": "gpsimd", "S": "sync"}
DQ = {"S": "dS", "G": "dG"}


class Buf:
    __slots__ = ("w", "rs")

    def __init__(self):
        self.w = None
        self.rs = []


class Ctx:
    def __init__(self, name, eng, sems):
        self.n = name
        self.e = eng
        self.sems = sems
        self.cnt = {s: 0 for s in STREAMS}
        self.lastw = {s: 0 for s in STREAMS}
        self.pending = []

    def _wait(self, tok):
        st, v = tok
        if EMAP.get(st) == self.n and self.n != "vector":
            return
        if st == "V" and self.n == "vector":
            v = v - 1 if False else v
        if v > self.lastw[st]:
            self.e.wait_ge(self.sems[st], v)
            self.lastw[st] = v

    def op(self, owner, fn, reads=(), writes=(), dma=False):
        deps = []
        for b in reads:
            if b.w is not None:
                deps.append(b.w)
        for b in writes:
            if b.w is not None:
                deps.append(b.w)
            deps.extend(b.rs)
        mine = (self.n == EMAP[owner])
        if mine:
            if self.pending:
                for t in self.pending:
                    self._wait(t)
                self.pending = []
            for t in sorted(set(deps)):
                self._wait(t)
        if dma:
            q = DQ[owner]
            self.cnt[q] += 16
            tok = (q, self.cnt[q])
            if mine:
                fn(self.e).then_inc(self.sems[q], 16)
        else:
            self.cnt[owner] += 1
            tok = (owner, self.cnt[owner])
            if mine:
                fn(self.e).then_inc(self.sems[owner], 1)
        for b in reads:
            b.rs.append(tok)
        for b in writes:
            b.w = tok
            b.rs = []
        return tok

    def barrier(self):
        self.pending = [(s, self.cnt[s]) for s in STREAMS if self.cnt[s] > 0]

    def raw_wait(self, owner, stream):
        if self.n == EMAP[owner]:
            v = self.cnt[stream]
            if v > self.lastw[stream]:
                self.e.wait_ge(self.sems[stream], v)
                self.lastw[stream] = v


def build_program():
    nc = bass.Bass("TRN2", target_bir_lowering=False)

    def din(name, shape, dt=F32):
        return nc.dram_tensor(name, shape, dt, kind="ExternalInput").ap()

    def dint(name, shape, dt=F32):
        return nc.dram_tensor(name, shape, dt, kind="Internal").ap()

    x_fm = din("x_fm", [D, S])
    wtoT_d = din("wtoT", [D, 2 * F])
    wfromT_d = din("wfromT", [2 * F, D])
    w1T_d = din("w1T", [D, DF])
    w2T_d = din("w2T", [DF, D])
    wmod_d = din("wmod", [D, 6 * D])
    rw_d = din("rw", [D, E])
    w1e_d = din("w1e", [E, D, DF])
    w2e_d = din("w2e", [E, DF, D])
    crow_d = din("crow", [128, 8])
    vecs_d = din("vecs", [128, 76])
    ident_d = din("ident", [128, 128])
    ones1x_d = din("ones1x", [1, 128])
    onescol_d = din("onescol", [128, 1])
    u128_d = din("u128", [128, 128])
    iotabc_d = din("iotabc", [128, 512])
    iota8bc_d = din("iota8bc", [128, 8])
    prevmask_d = din("prevmask", [8, 1])
    tmaskc_d = din("tmaskc", [128, 16])
    epsc_d = din("epsc", [1, 1])
    ccol_d = din("ccol", [128, 4])

    dkind = "ExternalOutput" if DEBUG["on"] else "Internal"
    x2_d = nc.dram_tensor("x2d", [D, 4096], F32, kind=dkind).ap()
    h3_d = nc.dram_tensor("h3d", [D, 4096], F32, kind=dkind).ap()
    xmid_d = nc.dram_tensor("xmidd", [4096, D], F32, kind=dkind).ap()
    hbuck_d = nc.dram_tensor("hbuckd", [E * CAP, D], F32, kind=dkind).ap()
    ybuck_d = nc.dram_tensor("ybuckd", [E * CAP, D], F32, kind=dkind).ap()
    cntin_d = dint("cntin", [1, E])
    cntout_d = dint("cntout", [8, 1, E])
    out_d = nc.dram_tensor("out", [4096, D], F32, kind="ExternalOutput").ap()
    dbg = {}
    if DEBUG["on"]:
        for nm, shp in [("d_mod", [128, 48]), ("d_h", [D, 4096]), ("d_hh", [D, 4096]),
                        ("d_u", [512, 4096]), ("d_z", [512, 4096]),
                        ("d_h2", [D, 4096]), ("d_rout", [128, 160]),
                        ("d_dbg2", [128, 64]),
                        ("d_trows", [1, 8 * 512])]:
            dbg[nm] = nc.dram_tensor(nm, shp, F32, kind="ExternalOutput").ap()

    def PS(name):
        return nc.alloc_psum_tensor(name, [128, 512], F32).ap()

    psA, psB, psBC1, psBC2, psT, psS, psS2, psR = (
        PS("psA"), PS("psB"), PS("psBC1"), PS("psBC2"), PS("psT"),
        PS("psS"), PS("psS2"), PS("psR"))

    def SBA(name, shape, dt=F32):
        return nc.alloc_sbuf_tensor(name, shape, dt).ap()

    ident = SBA("c_ident", [128, 128])
    ones1x = SBA("c_ones1x", [1, 128])
    onescol = SBA("c_onescol", [128, 1])
    u128 = SBA("c_u128", [128, 128])
    iotabc = SBA("c_iotabc", [128, 512])
    iota8bc = SBA("c_iota8", [128, 8])
    prevmask = SBA("c_prevm", [8, 1])
    tmaskc = SBA("c_tmask", [128, 16])
    epsc = SBA("c_eps", [1, 1])
    ccol = SBA("c_ccol", [128, 4])
    vecs = SBA("c_vecs", [128, 76])
    crow = SBA("c_crow", [128, 8])
    silc = SBA("silc", [128, 8])
    mod_fm = SBA("mod_fm", [128, 48])
    capc = SBA("capc", [128, 8])
    cbpc = SBA("cbpc", [128, 8])
    gac = SBA("gac", [128, 8])
    gbc = SBA("gbc", [128, 8])
    gtmp = SBA("gtmp", [128, 8])
    statW4 = SBA("statW4", [128, 8, 4])
    statW2 = SBA("statW2", [128, 8, 2])
    mvec = SBA("mvec", [128, 40])
    mrow = SBA("mrow", [1, 16])
    dec2 = SBA("dec2", [128, 2])
    om2 = SBA("om2", [128, 2])
    decbc = SBA("decbc", [128, 2, 512])
    gbrow = SBA("gbrow", [1, 1024])
    gbbc = SBA("gbbc", [128, 1024])
    trows = SBA("trows", [1, 8, 512])
    bcrow = SBA("bcrow", [1, 512])
    sqt = SBA("sqt", [128, 512])
    cntrow = SBA("cntrow", [1, 8])
    cntall = SBA("cntall", [8, 8])
    brow = SBA("brow", [1, 8])
    basebc = SBA("basebc", [128, 8])
    slot_all = SBA("slot_all", [128, 32])
    eidx_all = SBA("eidx_all", [128, 32])
    gate_all = SBA("gate_all", [128, 32])
    mult_all = SBA("mult_all", [128, 32])
    offu_all = SBA("offu_all", [128, 32], U32)
    rt = SBA("rt", [128, 9, 8])
    rc = SBA("rc", [128, 6])
    wmtile = SBA("wmtile", [128, 2, 128])
    rwsb = SBA("rwsb", [128, 8, 8])

    def group(defs):
        cms, out = [], []
        for (name, shape) in defs:
            cm = nc.sbuf_tensor(name, shape, F32)
            out.append(cm.__enter__())
            cms.append(cm)
        for cm in reversed(cms):
            cm.__exit__(None, None, None)
        return out

    cmi = nc.sbuf_tensor("p1_tmpi", [128, 512], mybir.dt.int32)
    tmpi = cmi.__enter__()
    xt, hbuf, hh, upl, vpl, wpl, carry, zpl, tbl, tmp2, wtoT, wfromT = group([
        ("p1_xt", [128, 2, 8, 512]), ("p1_h", [128, 8, 512]),
        ("p1_hh", [128, 8, 512]), ("p1_u", [128, 4, 512]),
        ("p1_v", [128, 2, 512]), ("p1_w", [128, 4, 512]),
        ("p1_carry", [128, 4, 1]), ("p1_z", [128, 4, 512]),
        ("p1_tbl", [128, 4, 512]), ("p1_tmp", [128, 2, 512]),
        ("p1_wto", [128, 8, 512]), ("p1_wfrom", [128, 4, 1024])])

    cmi.__exit__(None, None, None)
    h3b, x2b, xb2, hid, w1sl, w2sl, h2b, tmb = group([
        ("p2_h3", [128, 8, 512]), ("p2_x2", [128, 8, 512]),
        ("p2_x", [128, 8, 512]), ("p2_hid", [128, 32, 512]),
        ("p2_w1", [128, 2, 8, 128]), ("p2_w2", [128, 1, 32, 128]),
        ("p2_h2", [128, 8, 512]), ("p2_tm", [128, 2, 1024])])

    hbtm, hbfm, hidm, ytm, yfmt, w1sl3, w2sl3 = group([
        ("p3_hbtm", [128, 2, 1024]), ("p3_hbfm", [128, 8, 640]),
        ("p3_hid", [128, 32, 640]), ("p3_ytm", [128, 5, 1024]),
        ("p3_yfmt", [128, 512]), ("p3_w1", [128, 2, 8, 128]),
        ("p3_w2", [128, 1, 32, 128])])

    xmtm, ygb, outb = group([
        ("p4_xmtm", [128, 2, 1024]), ("p4_ygb", [128, 2, 1024]),
        ("p4_out", [128, 2, 1024])])

    sems = {s: nc.alloc_semaphore(s) for s in STREAMS}

    def schedule(p: Ctx):
        bufs = {}

        def bf(*names):
            out = []
            for n in names:
                if n not in bufs:
                    bufs[n] = Buf()
                out.append(bufs[n])
            return out if len(out) > 1 else out[0]

        def T(fn, r=(), w=()):
            return p.op("T", fn, r, w)

        def V(fn, r=(), w=()):
            return p.op("V", fn, r, w)

        def C(fn, r=(), w=()):
            return p.op("C", fn, r, w)

        def G(fn, r=(), w=()):
            return p.op("G", fn, r, w)

        def dmaS(outap, inap, r=(), w=()):
            return p.op("S", lambda e: e.dma_start(out=outap, in_=inap), r, w, dma=True)

        def dmaG(outap, inap, r=(), w=()):
            return p.op("G", lambda e: e.dma_start(out=outap, in_=inap), r, w, dma=True)

        def idmaG(fn, r=(), w=()):
            return p.op("G", fn, r, w, dma=True)

        CONST, MROW = bf("CONST", "MROW")
        dmaS(ident[:, :], ident_d[:, :], w=[CONST])
        dmaS(ones1x[:, :], ones1x_d[:, :], w=[CONST])
        dmaS(onescol[:, :], onescol_d[:, :], w=[CONST])
        dmaS(u128[:, :], u128_d[:, :], w=[CONST])
        dmaS(iotabc[:, :], iotabc_d[:, :], w=[CONST])
        dmaS(iota8bc[:, :], iota8bc_d[:, :], w=[CONST])
        dmaS(prevmask[:, :], prevmask_d[:, :], w=[CONST])
        dmaS(tmaskc[:, :], tmaskc_d[:, :], w=[CONST])
        dmaS(epsc[:, :], epsc_d[:, :], w=[CONST])
        dmaS(ccol[:, :], ccol_d[:, :], w=[CONST])
        dmaS(vecs[:, :], vecs_d[:, :], w=[CONST])
        dmaS(crow[:, :], crow_d[:, :], w=[CONST])
        dmaS(rwsb[:, :, :], rw_d.rearrange("(c p) e -> p c e", p=128), w=[CONST])
        SILC, MODF = bf("SILC", "MODF")
        C(lambda e: e.activation(out=silc[:, :], in_=crow[:, :], func=AF.Silu),
          r=[CONST], w=[SILC])
        WMT = [bf("WMT0"), bf("WMT1")]
        PSAB, PSBB = bf("PSA", "PSB")
        for jt in range(48):
            for kc in range(8):
                sl = (jt * 8 + kc) % 2
                dmaS(wmtile[:, sl, :],
                     wmod_d[kc * 128:(kc + 1) * 128, jt * 128:(jt + 1) * 128],
                     w=[WMT[sl]])
                T(lambda e, sl=sl, kc=kc: e.matmul(
                    psB[:, 0:1], wmtile[:, sl, :], silc[:, kc:kc + 1],
                    start=(kc == 0), stop=(kc == 7)),
                  r=[WMT[sl], SILC], w=[PSBB])
            V(lambda e, jt=jt: e.tensor_copy(out=mod_fm[:, jt:jt + 1], in_=psB[:, 0:1]),
              r=[PSBB], w=[MODF])
        CAPC, CBPC, GAC, GBC, GTMP = bf("CAPC", "CBPC", "GAC", "GBC", "GTMP")
        DBGB = bf("DBGB")
        V(lambda e: e.tensor_scalar(out=capc[:, :], in0=mod_fm[:, 8:16],
                                    scalar1=1.0, scalar2=None, op0=AL.add),
          r=[MODF], w=[CAPC])
        V(lambda e: e.tensor_scalar(out=cbpc[:, :], in0=mod_fm[:, 32:40],
                                    scalar1=1.0, scalar2=None, op0=AL.add),
          r=[MODF], w=[CBPC])
        for (gi, (src0, dst, DST)) in enumerate(((16, gac, GAC), (40, gbc, GBC))):
            C(lambda e, s=src0: e.activation(out=gtmp[:, :], in_=mod_fm[:, s:s + 8],
                                             func=AF.Sign),
              r=[MODF], w=[GTMP])
            C(lambda e: e.activation(out=gtmp[:, :], in_=gtmp[:, :], func=AF.Sign,
                                     bias=ccol[:, 0:1]),
              r=[GTMP, CONST], w=[GTMP])
            if DEBUG["on"]:
                dmaG(dbg["d_dbg2"][:, gi * 32:gi * 32 + 8], gtmp[:, :],
                     r=[GTMP], w=[DBGB])
            C(lambda e, s=src0, d=dst: e.activation(out=d[:, :], in_=mod_fm[:, s:s + 8],
                                                    func=AF.Abs),
              r=[MODF], w=[DST])
            if DEBUG["on"]:
                dmaG(dbg["d_dbg2"][:, gi * 32 + 8:gi * 32 + 16], dst[:, :],
                     r=[DST], w=[DBGB])
            V(lambda e, d=dst: e.tensor_scalar(out=d[:, :], in0=d[:, :], scalar1=0.25,
                                               scalar2=None, op0=AL.max),
              r=[DST], w=[DST])
            if DEBUG["on"]:
                dmaG(dbg["d_dbg2"][:, gi * 32 + 16:gi * 32 + 24], dst[:, :],
                     r=[DST], w=[DBGB])
            V(lambda e, d=dst: e.tensor_tensor(out=d[:, :], in0=d[:, :], in1=gtmp[:, :],
                                               op=AL.mult),
              r=[DST, GTMP], w=[DST])
            if DEBUG["on"]:
                dmaG(dbg["d_dbg2"][:, gi * 32 + 24:gi * 32 + 32], dst[:, :],
                     r=[DST], w=[DBGB])
        DBGB = bf("DBGB")
        if DEBUG["on"]:
            dmaG(dbg["d_mod"][:, :], mod_fm[:, :], r=[MODF], w=[DBGB])
        SW4, SW2, MVEC = bf("SW4", "SW2", "MVEC")
        for c in range(8):
            V(lambda e, c=c: e.memset(statW4[:, c, 0:1], 1.0), w=[SW4])
            V(lambda e, c=c: e.memset(statW2[:, c, 0:1], 1.0), w=[SW2])
            V(lambda e, c=c: e.tensor_copy(out=statW4[:, c, 1:2], in_=capc[:, c:c + 1]),
              r=[CAPC], w=[SW4])
            V(lambda e, c=c: e.tensor_tensor(out=statW4[:, c, 2:3], in0=capc[:, c:c + 1],
                                             in1=capc[:, c:c + 1], op=AL.mult),
              r=[CAPC], w=[SW4])
            V(lambda e, c=c: e.tensor_tensor(out=statW4[:, c, 3:4], in0=capc[:, c:c + 1],
                                             in1=mod_fm[:, c:c + 1], op=AL.mult),
              r=[CAPC, MODF], w=[SW4])
            V(lambda e, c=c: e.tensor_copy(out=statW2[:, c, 1:2], in_=statW4[:, c, 2:3]),
              r=[SW4], w=[SW2])
            V(lambda e, c=c: e.tensor_copy(out=mvec[:, c:c + 1], in_=capc[:, c:c + 1]),
              r=[CAPC], w=[MVEC])
            V(lambda e, c=c: e.tensor_copy(out=mvec[:, 8 + c:9 + c], in_=statW4[:, c, 2:3]),
              r=[SW4], w=[MVEC])
            V(lambda e, c=c: e.tensor_copy(out=mvec[:, 16 + c:17 + c],
                                           in_=statW4[:, c, 3:4]),
              r=[SW4], w=[MVEC])
            V(lambda e, c=c: e.tensor_copy(out=mvec[:, 24 + c:25 + c],
                                           in_=mod_fm[:, c:c + 1]),
              r=[MODF], w=[MVEC])
            V(lambda e, c=c: e.tensor_tensor(out=mvec[:, 32 + c:33 + c],
                                             in0=mod_fm[:, c:c + 1],
                                             in1=mod_fm[:, c:c + 1], op=AL.mult),
              r=[MODF], w=[MVEC])
        PSRB = bf("PSR")
        T(lambda e: e.matmul(psR[0:1, 0:40], onescol[:, :], mvec[:, :],
                             start=True, stop=True),
          r=[CONST, MVEC], w=[PSRB])
        for i in range(5):
            V(lambda e, i=i: e.tensor_reduce(out=mrow[0:1, i:i + 1],
                                             in_=psR[0:1, i * 8:(i + 1) * 8],
                                             axis=AX.X, op=AL.add),
              r=[PSRB], w=[MROW])
        V(lambda e: e.tensor_scalar(out=mrow[0:1, 0:5], in0=mrow[0:1, 0:5],
                                    scalar1=INVD, scalar2=None, op0=AL.mult),
          r=[MROW], w=[MROW])
        DEC, OM, DECBC = bf("DEC", "OM", "DECBC")
        C(lambda e: e.activation(out=dec2[:, :], in_=vecs[:, 40:42], func=AF.Sigmoid),
          r=[CONST], w=[DEC])
        C(lambda e: e.activation(out=om2[:, :], in_=vecs[:, 42:44], func=AF.Tanh),
          r=[CONST], w=[OM])
        V(lambda e: e.tensor_scalar(out=om2[:, :], in0=om2[:, :], scalar1=0.1,
                                    scalar2=None, op0=AL.mult), r=[OM], w=[OM])
        for c in range(2):
            V(lambda e, c=c: e.tensor_scalar(out=decbc[:, c, :], in0=iotabc[:, :],
                                             scalar1=0.0, scalar2=dec2[:, c:c + 1],
                                             op0=AL.mult, op1=AL.add),
              r=[CONST, DEC], w=[DECBC])
        GBROW, GBBC, PSTB, PSBC1B, PSBC2B = bf("GBROW", "GBBC", "PST", "PSBC1", "PSBC2")
        for c in range(8):
            T(lambda e, c=c: e.transpose(psT[0:1, 0:128], gbc[:, c:c + 1], ident[:, :]),
              r=[GBC, CONST], w=[PSTB])
            V(lambda e, c=c: e.tensor_copy(out=gbrow[0:1, c * 128:(c + 1) * 128],
                                           in_=psT[0:1, 0:128]),
              r=[PSTB], w=[GBROW])
        for half in range(2):
            T(lambda e, half=half: e.matmul(psBC1[:, :], ones1x[:, :],
                                            gbrow[0:1, half * 512:(half + 1) * 512],
                                            start=True, stop=True),
              r=[CONST, GBROW], w=[PSBC1B])
            V(lambda e, half=half: e.tensor_copy(out=gbbc[:, half * 512:(half + 1) * 512],
                                                 in_=psBC1[:, :]),
              r=[PSBC1B], w=[GBBC])
        WTO, WFR, CNT = bf("WTO", "WFR", "CNTROW")
        dmaS(wtoT[:, :, :], wtoT_d.rearrange("(c p) f -> p c f", p=128), w=[WTO])
        dmaS(wfromT[:, :, :], wfromT_d.rearrange("(c p) d -> p c d", p=128), w=[WFR])
        V(lambda e: e.memset(cntrow[0:1, :], 0.0), w=[CNT])

        XT = [bf("XT0"), bf("XT1")]
        HB, HH, UPL, VPL, WPL, ZPL, TBL, TMP2, SQT = bf(
            "HB", "HH", "UPL", "VPL", "WPL", "ZPL", "TBL", "TMP2", "SQT")
        CARRY = bf("CARRY")
        TMPI = bf("TMPI")
        TRW, BCROW = bf("TRW", "BCROW")
        PSSB, PSS2B = bf("PSS", "PSS2")
        X2D, H3D = bf("X2D", "H3D")

        def ldx(tile):
            sl = tile % 2
            dmaS(xt[:, sl, :, :],
                 x_fm[:, tile * 512:(tile + 1) * 512].rearrange(
                     "(c p) t -> p c t", p=128),
                 w=[XT[sl]])

        def stats_ln_simple(src, SRC):
            for c in range(8):
                C(lambda e, c=c: e.activation(out=sqt[:, :], in_=src[:, c, :],
                                              func=AF.Square),
                  r=[SRC], w=[SQT])
                T(lambda e, c=c: e.matmul(psS[0:1, :], onescol[:, :], src[:, c, :],
                                          start=(c == 0), stop=(c == 7)),
                  r=[CONST, SRC], w=[PSSB])
                T(lambda e, c=c: e.matmul(psS2[0:1, :], onescol[:, :], sqt[:, :],
                                          start=(c == 0), stop=(c == 7)),
                  r=[CONST, SQT], w=[PSS2B])
            V(lambda e: e.tensor_scalar(out=trows[0:1, 0, :], in0=psS[0:1, :],
                                        scalar1=INVD, scalar2=None, op0=AL.mult),
              r=[PSSB], w=[TRW])
            V(lambda e: e.tensor_scalar(out=trows[0:1, 5, :], in0=psS2[0:1, :],
                                        scalar1=INVD, scalar2=None, op0=AL.mult),
              r=[PSS2B], w=[TRW])
            V(lambda e: e.tensor_tensor(out=trows[0:1, 6, :], in0=trows[0:1, 0, :],
                                        in1=trows[0:1, 0, :], op=AL.mult),
              r=[TRW], w=[TRW])
            V(lambda e: e.tensor_tensor(out=trows[0:1, 5, :], in0=trows[0:1, 5, :],
                                        in1=trows[0:1, 6, :], op=AL.subtract),
              r=[TRW], w=[TRW])
            C(lambda e: e.activation(out=trows[0:1, 5, :], in_=trows[0:1, 5, :],
                                     func=AF.Sqrt, bias=epsc[0:1, 0:1]),
              r=[TRW, CONST], w=[TRW])
            V(lambda e: e.reciprocal(out=trows[0:1, 1, :], in_=trows[0:1, 5, :]),
              r=[TRW], w=[TRW])

        def bcast(psum, PSUMB, row):
            V(lambda e, row=row: e.tensor_copy(out=bcrow[0:1, :],
                                               in_=trows[0:1, row, :]),
              r=[TRW], w=[BCROW])
            T(lambda e, psum=psum: e.matmul(psum[:, :], ones1x[:, :], bcrow[0:1, :],
                                            start=True, stop=True),
              r=[CONST, BCROW], w=[PSUMB])

        ldx(0)
        for tile in range(NTF):
            sl = tile % 2
            if tile + 1 < NTF:
                ldx(tile + 1)
            for c in range(8):
                C(lambda e, sl=sl, c=c: e.activation(out=sqt[:, :], in_=xt[:, sl, c, :],
                                                     func=AF.Square),
                  r=[XT[sl]], w=[SQT])
                T(lambda e, sl=sl, c=c: e.matmul(psS[0:1, :], statW4[:, c, 0:1],
                                                 xt[:, sl, c, :],
                                                 start=(c == 0), stop=(c == 7)),
                  r=[SW4, XT[sl]], w=[PSSB])
                T(lambda e, sl=sl, c=c: e.matmul(psBC1[0:1, :], statW4[:, c, 1:2],
                                                 xt[:, sl, c, :],
                                                 start=(c == 0), stop=(c == 7)),
                  r=[SW4, XT[sl]], w=[PSBC1B])
                T(lambda e, sl=sl, c=c: e.matmul(psBC2[0:1, :], statW4[:, c, 2:3],
                                                 xt[:, sl, c, :],
                                                 start=(c == 0), stop=(c == 7)),
                  r=[SW4, XT[sl]], w=[PSBC2B])
                T(lambda e, sl=sl, c=c: e.matmul(psR[0:1, 0:512], statW4[:, c, 3:4],
                                                 xt[:, sl, c, :],
                                                 start=(c == 0), stop=(c == 7)),
                  r=[SW4, XT[sl]], w=[PSRB])
                T(lambda e, c=c: e.matmul(psS2[0:1, :], statW2[:, c, 0:1], sqt[:, :],
                                          start=(c == 0), stop=(c == 7)),
                  r=[SW2, SQT], w=[PSS2B])
                T(lambda e, c=c: e.matmul(psT[0:1, :], statW2[:, c, 1:2], sqt[:, :],
                                          start=(c == 0), stop=(c == 7)),
                  r=[SW2, SQT], w=[PSTB])
            V(lambda e: e.tensor_scalar(out=trows[0:1, 0, :], in0=psS[0:1, :],
                                        scalar1=INVD, scalar2=None, op0=AL.mult),
              r=[PSSB], w=[TRW])
            V(lambda e: e.tensor_scalar(out=trows[0:1, 5, :], in0=psS2[0:1, :],
                                        scalar1=INVD, scalar2=None, op0=AL.mult),
              r=[PSS2B], w=[TRW])
            V(lambda e: e.tensor_tensor(out=trows[0:1, 6, :], in0=trows[0:1, 0, :],
                                        in1=trows[0:1, 0, :], op=AL.mult),
              r=[TRW], w=[TRW])
            V(lambda e: e.tensor_tensor(out=trows[0:1, 5, :], in0=trows[0:1, 5, :],
                                        in1=trows[0:1, 6, :], op=AL.subtract),
              r=[TRW], w=[TRW])
            C(lambda e: e.activation(out=trows[0:1, 5, :], in_=trows[0:1, 5, :],
                                     func=AF.Sqrt, bias=epsc[0:1, 0:1]),
              r=[TRW, CONST], w=[TRW])
            V(lambda e: e.reciprocal(out=trows[0:1, 1, :], in_=trows[0:1, 5, :]),
              r=[TRW], w=[TRW])
            V(lambda e: e.scalar_tensor_tensor(out=trows[0:1, 2, :], in0=trows[0:1, 0, :],
                                               scalar=-1.0, in1=trows[0:1, 1, :],
                                               op0=AL.mult, op1=AL.mult),
              r=[TRW], w=[TRW])
            V(lambda e: e.tensor_scalar(out=trows[0:1, 5, :], in0=psBC1[0:1, :],
                                        scalar1=INVD, scalar2=None, op0=AL.mult),
              r=[PSBC1B], w=[TRW])
            V(lambda e: e.tensor_scalar(out=trows[0:1, 6, :], in0=trows[0:1, 0, :],
                                        scalar1=mrow[0:1, 0:1], scalar2=None,
                                        op0=AL.mult),
              r=[TRW, MROW], w=[TRW])
            V(lambda e: e.tensor_tensor(out=trows[0:1, 5, :], in0=trows[0:1, 5, :],
                                        in1=trows[0:1, 6, :], op=AL.subtract),
              r=[TRW], w=[TRW])
            V(lambda e: e.tensor_tensor(out=trows[0:1, 5, :], in0=trows[0:1, 5, :],
                                        in1=trows[0:1, 1, :], op=AL.mult),
              r=[TRW], w=[TRW])
            V(lambda e: e.tensor_scalar(out=trows[0:1, 3, :], in0=trows[0:1, 5, :],
                                        scalar1=mrow[0:1, 3:4], scalar2=None,
                                        op0=AL.add),
              r=[TRW, MROW], w=[TRW])
            V(lambda e: e.tensor_scalar(out=trows[0:1, 5, :], in0=psBC2[0:1, :],
                                        scalar1=2.0 * INVD, scalar2=None, op0=AL.mult),
              r=[PSBC2B], w=[TRW])
            V(lambda e: e.tensor_tensor(out=trows[0:1, 5, :], in0=trows[0:1, 0, :],
                                        in1=trows[0:1, 5, :], op=AL.mult),
              r=[TRW], w=[TRW])
            V(lambda e: e.tensor_scalar(out=trows[0:1, 6, :], in0=psT[0:1, :],
                                        scalar1=INVD, scalar2=None, op0=AL.mult),
              r=[PSTB], w=[TRW])
            V(lambda e: e.tensor_tensor(out=trows[0:1, 5, :], in0=trows[0:1, 6, :],
                                        in1=trows[0:1, 5, :], op=AL.subtract),
              r=[TRW], w=[TRW])
            V(lambda e: e.tensor_tensor(out=trows[0:1, 6, :], in0=trows[0:1, 0, :],
                                        in1=trows[0:1, 0, :], op=AL.mult),
              r=[TRW], w=[TRW])
            V(lambda e: e.tensor_scalar(out=trows[0:1, 6, :], in0=trows[0:1, 6, :],
                                        scalar1=mrow[0:1, 1:2], scalar2=None,
                                        op0=AL.mult),
              r=[TRW, MROW], w=[TRW])
            V(lambda e: e.tensor_tensor(out=trows[0:1, 5, :], in0=trows[0:1, 5, :],
                                        in1=trows[0:1, 6, :], op=AL.add),
              r=[TRW], w=[TRW])
            V(lambda e: e.tensor_tensor(out=trows[0:1, 6, :], in0=trows[0:1, 1, :],
                                        in1=trows[0:1, 1, :], op=AL.mult),
              r=[TRW], w=[TRW])
            V(lambda e: e.tensor_tensor(out=trows[0:1, 5, :], in0=trows[0:1, 5, :],
                                        in1=trows[0:1, 6, :], op=AL.mult),
              r=[TRW], w=[TRW])
            V(lambda e: e.tensor_scalar(out=trows[0:1, 6, :], in0=psR[0:1, 0:512],
                                        scalar1=INVD, scalar2=None, op0=AL.mult),
              r=[PSRB], w=[TRW])
            V(lambda e: e.tensor_scalar(out=trows[0:1, 7, :], in0=trows[0:1, 0, :],
                                        scalar1=mrow[0:1, 2:3], scalar2=None,
                                        op0=AL.mult),
              r=[TRW, MROW], w=[TRW])
            V(lambda e: e.tensor_tensor(out=trows[0:1, 6, :], in0=trows[0:1, 6, :],
                                        in1=trows[0:1, 7, :], op=AL.subtract),
              r=[TRW], w=[TRW])
            V(lambda e: e.tensor_tensor(out=trows[0:1, 6, :], in0=trows[0:1, 6, :],
                                        in1=trows[0:1, 1, :], op=AL.mult),
              r=[TRW], w=[TRW])
            V(lambda e: e.tensor_scalar(out=trows[0:1, 6, :], in0=trows[0:1, 6, :],
                                        scalar1=2.0, scalar2=None, op0=AL.mult),
              r=[TRW], w=[TRW])
            V(lambda e: e.tensor_tensor(out=trows[0:1, 5, :], in0=trows[0:1, 5, :],
                                        in1=trows[0:1, 6, :], op=AL.add),
              r=[TRW], w=[TRW])
            V(lambda e: e.tensor_scalar(out=trows[0:1, 5, :], in0=trows[0:1, 5, :],
                                        scalar1=mrow[0:1, 4:5], scalar2=None,
                                        op0=AL.add),
              r=[TRW, MROW], w=[TRW])
            V(lambda e: e.tensor_tensor(out=trows[0:1, 6, :], in0=trows[0:1, 3, :],
                                        in1=trows[0:1, 3, :], op=AL.mult),
              r=[TRW], w=[TRW])
            V(lambda e: e.tensor_tensor(out=trows[0:1, 5, :], in0=trows[0:1, 5, :],
                                        in1=trows[0:1, 6, :], op=AL.subtract),
              r=[TRW], w=[TRW])
            C(lambda e: e.activation(out=trows[0:1, 5, :], in_=trows[0:1, 5, :],
                                     func=AF.Sqrt, bias=epsc[0:1, 0:1]),
              r=[TRW, CONST], w=[TRW])
            V(lambda e: e.reciprocal(out=trows[0:1, 4, :], in_=trows[0:1, 5, :]),
              r=[TRW], w=[TRW])
            bcast(psBC1, PSBC1B, 1)
            bcast(psBC2, PSBC2B, 2)
            for c in range(8):
                V(lambda e, sl=sl, c=c: e.tensor_tensor(out=hbuf[:, c, :],
                                                        in0=xt[:, sl, c, :],
                                                        in1=psBC1[:, :], op=AL.mult),
                  r=[XT[sl], PSBC1B], w=[HB])
                V(lambda e, c=c: e.tensor_tensor(out=hbuf[:, c, :], in0=hbuf[:, c, :],
                                                 in1=psBC2[:, :], op=AL.add),
                  r=[PSBC2B], w=[HB])
                V(lambda e, c=c: e.tensor_scalar(out=hbuf[:, c, :], in0=hbuf[:, c, :],
                                                 scalar1=capc[:, c:c + 1],
                                                 scalar2=mod_fm[:, c:c + 1],
                                                 op0=AL.mult, op1=AL.add),
                  r=[CAPC, MODF], w=[HB])
            bcast(psBC1, PSBC1B, 3)
            bcast(psBC2, PSBC2B, 4)
            for c in range(8):
                V(lambda e, c=c: e.tensor_tensor(out=hh[:, c, :], in0=hbuf[:, c, :],
                                                 in1=psBC1[:, :], op=AL.subtract),
                  r=[HB, PSBC1B], w=[HH])
                V(lambda e, c=c: e.tensor_tensor(out=hh[:, c, :], in0=hh[:, c, :],
                                                 in1=psBC2[:, :], op=AL.mult),
                  r=[PSBC2B], w=[HH])
                V(lambda e, c=c: e.tensor_scalar(out=hh[:, c, :], in0=hh[:, c, :],
                                                 scalar1=vecs[:, 0 + c:1 + c],
                                                 scalar2=vecs[:, 8 + c:9 + c],
                                                 op0=AL.mult, op1=AL.add),
                  r=[CONST], w=[HH])
            for fc in range(4):
                ps = psA if fc % 2 == 0 else psB
                PSX = PSAB if fc % 2 == 0 else PSBB
                for c in range(8):
                    T(lambda e, fc=fc, c=c, ps=ps: e.matmul(
                        ps[:, :], wtoT[:, c, fc * 128:(fc + 1) * 128], hh[:, c, :],
                        start=(c == 0), stop=(c == 7)),
                      r=[WTO, HH], w=[PSX])
                C(lambda e, fc=fc, ps=ps, tile=tile: e.activation(
                    out=upl[:, fc, :], in_=ps[:, :], func=AF.Copy,
                    scale=tmaskc[:, tile:tile + 1]),
                  r=[PSX, CONST], w=[UPL])
            t0 = float(tile * 512)
            for c in range(2):
                def redsin(dst_plane, addpi2, addom, c=c, t0=t0):
                    V(lambda e: e.tensor_scalar(
                        out=tmp2[:, 0, :], in0=iotabc[:, :], scalar1=t0,
                        scalar2=om2[:, c:c + 1], op0=AL.add, op1=AL.mult),
                      r=[CONST, OM], w=[TMP2])
                    if addom:
                        V(lambda e: e.tensor_scalar(
                            out=tmp2[:, 0, :], in0=tmp2[:, 0, :],
                            scalar1=om2[:, c:c + 1], scalar2=None, op0=AL.add),
                          r=[OM, TMP2], w=[TMP2])
                    if addpi2:
                        V(lambda e: e.tensor_scalar(
                            out=tmp2[:, 0, :], in0=tmp2[:, 0, :],
                            scalar1=PI / 2.0, scalar2=None, op0=AL.add),
                          r=[TMP2], w=[TMP2])
                    V(lambda e: e.tensor_scalar(out=tmp2[:, 1, :], in0=tmp2[:, 0, :],
                                                scalar1=1.0 / TWO_PI, scalar2=None,
                                                op0=AL.mult),
                      r=[TMP2], w=[TMP2])
                    V(lambda e: e.tensor_copy(out=tmpi[:, :], in_=tmp2[:, 1, :]),
                      r=[TMP2], w=[TMPI])
                    V(lambda e: e.tensor_copy(out=tmp2[:, 0, :], in_=tmpi[:, :]),
                      r=[TMPI], w=[TMP2])
                    V(lambda e: e.tensor_tensor(out=tmp2[:, 1, :], in0=tmp2[:, 1, :],
                                                in1=tmp2[:, 0, :], op=AL.subtract),
                      r=[TMP2], w=[TMP2])
                    V(lambda e: e.tensor_scalar(out=tmp2[:, 0, :], in0=tmp2[:, 1, :],
                                                scalar1=0.5, scalar2=None,
                                                op0=AL.is_gt),
                      r=[TMP2], w=[TMP2])
                    V(lambda e: e.tensor_tensor(out=tmp2[:, 1, :], in0=tmp2[:, 1, :],
                                                in1=tmp2[:, 0, :], op=AL.subtract),
                      r=[TMP2], w=[TMP2])
                    V(lambda e: e.tensor_scalar(out=tmp2[:, 0, :], in0=tmp2[:, 1, :],
                                                scalar1=-0.5, scalar2=None,
                                                op0=AL.is_lt),
                      r=[TMP2], w=[TMP2])
                    V(lambda e: e.tensor_tensor(out=tmp2[:, 1, :], in0=tmp2[:, 1, :],
                                                in1=tmp2[:, 0, :], op=AL.add),
                      r=[TMP2], w=[TMP2])
                    V(lambda e: e.tensor_scalar(out=tmp2[:, 1, :], in0=tmp2[:, 1, :],
                                                scalar1=TWO_PI, scalar2=None,
                                                op0=AL.mult),
                      r=[TMP2], w=[TMP2])
                    C(lambda e, dp=dst_plane: e.activation(out=tbl[:, dp, :],
                                                           in_=tmp2[:, 1, :],
                                                           func=AF.Sin),
                      r=[TMP2], w=[TBL])

                redsin(0, True, False)
                redsin(1, False, False)
                V(lambda e, c=c: e.tensor_tensor(out=vpl[:, 0, :], in0=tbl[:, 0, :],
                                                 in1=upl[:, c, :], op=AL.mult),
                  r=[TBL, UPL], w=[VPL])
                V(lambda e, c=c: e.tensor_tensor(out=tmp2[:, 0, :], in0=tbl[:, 1, :],
                                                 in1=upl[:, 2 + c, :], op=AL.mult),
                  r=[TBL, UPL], w=[TMP2])
                V(lambda e: e.tensor_tensor(out=vpl[:, 0, :], in0=vpl[:, 0, :],
                                            in1=tmp2[:, 0, :], op=AL.add),
                  r=[TMP2], w=[VPL])
                V(lambda e, c=c: e.tensor_tensor(out=vpl[:, 1, :], in0=tbl[:, 0, :],
                                                 in1=upl[:, 2 + c, :], op=AL.mult),
                  r=[TBL, UPL], w=[VPL])
                V(lambda e, c=c: e.tensor_tensor(out=tmp2[:, 0, :], in0=tbl[:, 1, :],
                                                 in1=upl[:, c, :], op=AL.mult),
                  r=[TBL, UPL], w=[TMP2])
                V(lambda e: e.tensor_tensor(out=vpl[:, 1, :], in0=vpl[:, 1, :],
                                            in1=tmp2[:, 0, :], op=AL.subtract),
                  r=[TMP2], w=[VPL])
                for (pl, vv) in ((c, 0), (2 + c, 1)):
                    if tile == 0:
                        V(lambda e, pl=pl, vv=vv, c=c: e.tensor_tensor_scan(
                            out=wpl[:, pl, :], data0=decbc[:, c, :],
                            data1=vpl[:, vv, :], initial=0.0,
                            op0=AL.mult, op1=AL.add),
                          r=[DECBC, VPL], w=[WPL])
                    else:
                        V(lambda e, pl=pl, vv=vv, c=c: e.tensor_tensor_scan(
                            out=wpl[:, pl, :], data0=decbc[:, c, :],
                            data1=vpl[:, vv, :], initial=carry[:, pl, 0:1],
                            op0=AL.mult, op1=AL.add),
                          r=[DECBC, VPL, CARRY], w=[WPL])
                redsin(2, True, True)
                redsin(3, False, True)
                V(lambda e, c=c: e.tensor_tensor(out=zpl[:, c, :], in0=tbl[:, 2, :],
                                                 in1=wpl[:, c, :], op=AL.mult),
                  r=[TBL, WPL], w=[ZPL])
                V(lambda e, c=c: e.tensor_tensor(out=tmp2[:, 0, :], in0=tbl[:, 3, :],
                                                 in1=wpl[:, 2 + c, :], op=AL.mult),
                  r=[TBL, WPL], w=[TMP2])
                V(lambda e, c=c: e.tensor_tensor(out=zpl[:, c, :], in0=zpl[:, c, :],
                                                 in1=tmp2[:, 0, :], op=AL.subtract),
                  r=[TMP2], w=[ZPL])
                V(lambda e, c=c: e.tensor_tensor(out=zpl[:, 2 + c, :], in0=tbl[:, 3, :],
                                                 in1=wpl[:, c, :], op=AL.mult),
                  r=[TBL, WPL], w=[ZPL])
                V(lambda e, c=c: e.tensor_tensor(out=tmp2[:, 0, :], in0=tbl[:, 2, :],
                                                 in1=wpl[:, 2 + c, :], op=AL.mult),
                  r=[TBL, WPL], w=[TMP2])
                V(lambda e, c=c: e.tensor_tensor(out=zpl[:, 2 + c, :],
                                                 in0=zpl[:, 2 + c, :],
                                                 in1=tmp2[:, 0, :], op=AL.add),
                  r=[TMP2], w=[ZPL])
                for pl in (c, 2 + c):
                    V(lambda e, pl=pl: e.tensor_scalar(out=zpl[:, pl, :],
                                                       in0=zpl[:, pl, :],
                                                       scalar1=10.0, scalar2=-10.0,
                                                       op0=AL.min, op1=AL.max),
                      r=[ZPL], w=[ZPL])
                for pl in (c, 2 + c):
                    V(lambda e, pl=pl: e.tensor_copy(out=carry[:, pl, 0:1],
                                                     in_=wpl[:, pl, 511:512]),
                      r=[WPL], w=[CARRY])
            if tile < 8:
                continue
            ot = tile - 8
            if DEBUG["on"]:
                dmaG(dbg["d_h"][:, ot * 512:(ot + 1) * 512].rearrange(
                         "(c p) t -> p c t", p=128), hbuf[:, :, :], r=[HB], w=[DBGB])
                dmaG(dbg["d_hh"][:, ot * 512:(ot + 1) * 512].rearrange(
                         "(c p) t -> p c t", p=128), hh[:, :, :], r=[HH], w=[DBGB])
                dmaG(dbg["d_u"][:, ot * 512:(ot + 1) * 512].rearrange(
                         "(c p) t -> p c t", p=128), upl[:, :, :], r=[UPL], w=[DBGB])
                dmaG(dbg["d_z"][:, ot * 512:(ot + 1) * 512].rearrange(
                         "(c p) t -> p c t", p=128), zpl[:, :, :], r=[ZPL], w=[DBGB])
                if tile == 8:
                    dmaG(dbg["d_trows"][0:1, :],
                         trows[0:1, :, :].rearrange("o r t -> o (r t)"),
                         r=[TRW], w=[DBGB])
            for dc in range(8):
                ps = psA if dc % 2 == 0 else psB
                PSX = PSAB if dc % 2 == 0 else PSBB
                for fc in range(4):
                    T(lambda e, dc=dc, fc=fc, ps=ps: e.matmul(
                        ps[:, :], wfromT[:, fc, dc * 128:(dc + 1) * 128],
                        zpl[:, fc, :], start=(fc == 0), stop=(fc == 3)),
                      r=[WFR, ZPL], w=[PSX])
                V(lambda e, dc=dc, ps=ps: e.scalar_tensor_tensor(
                    out=hbuf[:, dc, :], in0=ps[:, :], scalar=0.1, in1=hbuf[:, dc, :],
                    op0=AL.mult, op1=AL.add),
                  r=[PSX], w=[HB])
            dmaG(x2_d[:, ot * 512:(ot + 1) * 512].rearrange("(c p) t -> p c t", p=128),
                 hbuf[:, :, :], r=[HB], w=[X2D])
            stats_ln_simple(hbuf, HB)
            bcast(psBC1, PSBC1B, 0)
            bcast(psBC2, PSBC2B, 1)
            for c in range(8):
                V(lambda e, c=c: e.tensor_tensor(out=hh[:, c, :], in0=hbuf[:, c, :],
                                                 in1=psBC1[:, :], op=AL.subtract),
                  r=[HB, PSBC1B], w=[HH])
                V(lambda e, c=c: e.tensor_tensor(out=hh[:, c, :], in0=hh[:, c, :],
                                                 in1=psBC2[:, :], op=AL.mult),
                  r=[PSBC2B], w=[HH])
                V(lambda e, c=c: e.tensor_scalar(out=hh[:, c, :], in0=hh[:, c, :],
                                                 scalar1=vecs[:, 16 + c:17 + c],
                                                 scalar2=vecs[:, 24 + c:25 + c],
                                                 op0=AL.mult, op1=AL.add),
                  r=[CONST], w=[HH])
            dmaG(h3_d[:, ot * 512:(ot + 1) * 512].rearrange("(c p) t -> p c t", p=128),
                 hh[:, :, :], r=[HH], w=[H3D])

        p.barrier()
        H3B, X2B, XB2, HIDB, H2B, TMB = bf("H3B", "X2B", "XB2", "HIDB", "H2B", "TMB")
        W1S = [bf("W1S0"), bf("W1S1")]
        W2S = [bf("W2S0"), bf("W2S1")]
        XMIDD, HBUCKD = bf("XMIDD", "HBUCKD")
        SLA, EIA, GAA, MUA, OFA = bf("SLA", "EIA", "GAA", "MUA", "OFA")
        RTB, RCB = bf("RTB", "RCB")

        for blk in range(8):
            c0 = blk * 512
            dmaS(h3b[:, :, :],
                 h3_d[:, c0:c0 + 512].rearrange("(c p) t -> p c t", p=128),
                 r=[H3D], w=[H3B])
            dmaS(x2b[:, :, :],
                 x2_d[:, c0:c0 + 512].rearrange("(c p) t -> p c t", p=128),
                 r=[X2D], w=[X2B])
            dmaS(xb2[:, :, :],
                 x_fm[:, 4096 + c0:4096 + c0 + 512].rearrange("(c p) t -> p c t", p=128),
                 w=[XB2])
            dmaS(w1sl[:, 0, :, :],
                 w1T_d[:, 0:128].rearrange("(c p) f -> p c f", p=128), w=[W1S[0]])
            for fc in range(32):
                sl = fc % 2
                if fc + 1 < 32:
                    nsl = (fc + 1) % 2
                    dmaS(w1sl[:, nsl, :, :],
                         w1T_d[:, (fc + 1) * 128:(fc + 2) * 128].rearrange(
                             "(c p) f -> p c f", p=128),
                         w=[W1S[nsl]])
                ps = psA if fc % 2 == 0 else psB
                PSX = PSAB if fc % 2 == 0 else PSBB
                for c in range(8):
                    T(lambda e, sl=sl, c=c, ps=ps: e.matmul(
                        ps[:, :], w1sl[:, sl, c, :], h3b[:, c, :],
                        start=(c == 0), stop=(c == 7)),
                      r=[W1S[sl], H3B], w=[PSX])
                C(lambda e, fc=fc, ps=ps: e.activation(
                    out=hid[:, fc, :], in_=ps[:, :], func=AF.Gelu,
                    bias=vecs[:, 44 + fc:45 + fc]),
                  r=[PSX, CONST], w=[HIDB])
            for dc in range(8):
                dmaS(w2sl[:, 0, :, :],
                     w2T_d[:, dc * 128:(dc + 1) * 128].rearrange(
                         "(c p) d -> p c d", p=128),
                     w=[W2S[0]])
                ps = psA if dc % 2 == 0 else psB
                PSX = PSAB if dc % 2 == 0 else PSBB
                for fc in range(32):
                    T(lambda e, fc=fc, ps=ps: e.matmul(
                        ps[:, :], w2sl[:, 0, fc, :], hid[:, fc, :],
                        start=(fc == 0), stop=(fc == 31)),
                      r=[W2S[0], HIDB], w=[PSX])
                V(lambda e, dc=dc, ps=ps: e.tensor_scalar(
                    out=sqt[:, :], in0=ps[:, :],
                    scalar1=vecs[:, 32 + dc:33 + dc], scalar2=0.1,
                    op0=AL.add, op1=AL.mult),
                  r=[PSX, CONST], w=[SQT])
                V(lambda e, dc=dc: e.tensor_tensor(out=sqt[:, :], in0=sqt[:, :],
                                                   in1=x2b[:, dc, :], op=AL.add),
                  r=[X2B], w=[SQT])
                V(lambda e, dc=dc: e.scalar_tensor_tensor(
                    out=xb2[:, dc, :], in0=sqt[:, :], scalar=gac[:, dc:dc + 1],
                    in1=xb2[:, dc, :], op0=AL.mult, op1=AL.add),
                  r=[SQT, GAC], w=[XB2])
            stats_ln_simple(xb2, XB2)
            bcast(psBC1, PSBC1B, 0)
            bcast(psBC2, PSBC2B, 1)
            for c in range(8):
                V(lambda e, c=c: e.tensor_tensor(out=h2b[:, c, :], in0=xb2[:, c, :],
                                                 in1=psBC1[:, :], op=AL.subtract),
                  r=[XB2, PSBC1B], w=[H2B])
                V(lambda e, c=c: e.tensor_tensor(out=h2b[:, c, :], in0=h2b[:, c, :],
                                                 in1=psBC2[:, :], op=AL.mult),
                  r=[PSBC2B], w=[H2B])
                V(lambda e, c=c: e.tensor_scalar(out=h2b[:, c, :], in0=h2b[:, c, :],
                                                 scalar1=cbpc[:, c:c + 1],
                                                 scalar2=mod_fm[:, 24 + c:25 + c],
                                                 op0=AL.mult, op1=AL.add),
                  r=[CBPC, MODF], w=[H2B])
            if DEBUG["on"]:
                dmaG(dbg["d_h2"][:, c0:c0 + 512].rearrange("(c p) t -> p c t", p=128),
                     h2b[:, :, :], r=[H2B], w=[DBGB])
            for cg in range(4):
                g = blk * 4 + cg
                for c in range(8):
                    T(lambda e, c=c, cg=cg: e.matmul(
                        psR[:, 0:8], h2b[:, c, cg * 128:(cg + 1) * 128],
                        rwsb[:, c, :], start=(c == 0), stop=(c == 7)),
                      r=[H2B, CONST], w=[PSRB])
                V(lambda e: e.tensor_copy(out=rt[:, 0, :], in_=psR[:, 0:8]),
                  r=[PSRB], w=[RTB])
                V(lambda e: e.tensor_reduce(out=rc[:, 0:1], in_=rt[:, 0, :], axis=AX.X,
                                            op=AL.max),
                  r=[RTB], w=[RCB])
                V(lambda e: e.tensor_scalar(out=rt[:, 1, :], in0=rt[:, 0, :],
                                            scalar1=rc[:, 0:1], scalar2=None,
                                            op0=AL.is_ge),
                  r=[RTB, RCB], w=[RTB])
                V(lambda e: e.tensor_copy(out=rt[:, 2, :], in_=rt[:, 1, :]),
                  r=[RTB], w=[RTB])
                V(lambda e: e.tensor_tensor(out=rt[:, 3, 1:8], in0=rt[:, 2, 1:8],
                                            in1=rt[:, 2, 0:7], op=AL.add),
                  r=[RTB], w=[RTB])
                V(lambda e: e.tensor_copy(out=rt[:, 3, 0:1], in_=rt[:, 2, 0:1]),
                  r=[RTB], w=[RTB])
                V(lambda e: e.tensor_tensor(out=rt[:, 4, 2:8], in0=rt[:, 3, 2:8],
                                            in1=rt[:, 3, 0:6], op=AL.add),
                  r=[RTB], w=[RTB])
                V(lambda e: e.tensor_copy(out=rt[:, 4, 0:2], in_=rt[:, 3, 0:2]),
                  r=[RTB], w=[RTB])
                V(lambda e: e.tensor_tensor(out=rt[:, 5, 4:8], in0=rt[:, 4, 4:8],
                                            in1=rt[:, 4, 0:4], op=AL.add),
                  r=[RTB], w=[RTB])
                V(lambda e: e.tensor_copy(out=rt[:, 5, 0:4], in_=rt[:, 4, 0:4]),
                  r=[RTB], w=[RTB])
                C(lambda e: e.activation(out=rt[:, 6, :], in_=rt[:, 5, :],
                                         func=AF.Abs, bias=ccol[:, 2:3]),
                  r=[RTB, CONST], w=[RTB])
                C(lambda e: e.activation(out=rt[:, 6, :], in_=rt[:, 6, :],
                                         func=AF.Relu, scale=ccol[:, 2:3],
                                         bias=ccol[:, 1:2]),
                  r=[RTB, CONST], w=[RTB])
                V(lambda e: e.tensor_tensor(out=rt[:, 6, :], in0=rt[:, 6, :],
                                            in1=rt[:, 1, :], op=AL.mult),
                  r=[RTB], w=[RTB])
                V(lambda e: e.tensor_tensor(out=rt[:, 7, :], in0=rt[:, 6, :],
                                            in1=iota8bc[:, :], op=AL.mult),
                  r=[RTB, CONST], w=[RTB])
                V(lambda e, g=g: e.tensor_reduce(out=eidx_all[:, g:g + 1],
                                                 in_=rt[:, 7, :], axis=AX.X,
                                                 op=AL.add),
                  r=[RTB], w=[EIA])
                V(lambda e: e.tensor_scalar(out=rc[:, 1:2], in0=rc[:, 0:1],
                                            scalar1=-1.0, scalar2=None, op0=AL.mult),
                  r=[RCB], w=[RCB])
                C(lambda e: e.activation(out=rt[:, 7, :], in_=rt[:, 0, :], func=AF.Exp,
                                         bias=rc[:, 1:2]),
                  r=[RTB, RCB], w=[RTB])
                V(lambda e: e.tensor_reduce(out=rc[:, 2:3], in_=rt[:, 7, :], axis=AX.X,
                                            op=AL.add),
                  r=[RTB], w=[RCB])
                V(lambda e, g=g: e.reciprocal(out=gate_all[:, g:g + 1],
                                              in_=rc[:, 2:3]),
                  r=[RCB], w=[GAA])
                T(lambda e: e.matmul(psR[:, 8:16], u128[:, :], rt[:, 6, :],
                                     start=True, stop=True),
                  r=[CONST, RTB], w=[PSRB])
                V(lambda e: e.tensor_tensor(out=rt[:, 8, :], in0=psR[:, 8:16],
                                            in1=rt[:, 6, :], op=AL.mult),
                  r=[PSRB, RTB], w=[RTB])
                V(lambda e: e.tensor_reduce(out=rc[:, 3:4], in_=rt[:, 8, :], axis=AX.X,
                                            op=AL.add),
                  r=[RTB], w=[RCB])
                T(lambda e: e.matmul(psR[:, 16:24], ones1x[:, :], cntrow[0:1, :],
                                     start=True, stop=True),
                  r=[CONST, CNT], w=[PSRB])
                V(lambda e: e.tensor_tensor(out=rt[:, 7, :], in0=psR[:, 16:24],
                                            in1=rt[:, 6, :], op=AL.mult),
                  r=[PSRB, RTB], w=[RTB])
                V(lambda e: e.tensor_reduce(out=rc[:, 4:5], in_=rt[:, 7, :], axis=AX.X,
                                            op=AL.add),
                  r=[RTB], w=[RCB])
                V(lambda e: e.tensor_tensor(out=rc[:, 3:4], in0=rc[:, 3:4],
                                            in1=rc[:, 4:5], op=AL.add),
                  r=[RCB], w=[RCB])
                V(lambda e, g=g: e.tensor_scalar(out=slot_all[:, g:g + 1],
                                                 in0=rc[:, 3:4], scalar1=-1.0,
                                                 scalar2=None, op0=AL.add),
                  r=[RCB], w=[SLA])
                T(lambda e: e.matmul(psR[0:1, 24:32], onescol[:, :], rt[:, 6, :],
                                     start=True, stop=True),
                  r=[CONST, RTB], w=[PSRB])
                V(lambda e: e.tensor_tensor(out=cntrow[0:1, :], in0=cntrow[0:1, :],
                                            in1=psR[0:1, 24:32], op=AL.add),
                  r=[PSRB, CNT], w=[CNT])
                V(lambda e, g=g: e.tensor_scalar(out=rc[:, 5:6],
                                                 in0=slot_all[:, g:g + 1],
                                                 scalar1=float(CAP - 1), scalar2=None,
                                                 op0=AL.min),
                  r=[SLA], w=[RCB])
                V(lambda e, g=g: e.scalar_tensor_tensor(
                    out=rc[:, 5:6], in0=eidx_all[:, g:g + 1], scalar=float(CAP),
                    in1=rc[:, 5:6], op0=AL.mult, op1=AL.add),
                  r=[EIA, RCB], w=[RCB])
                V(lambda e, g=g: e.tensor_copy(out=offu_all[:, g:g + 1],
                                               in_=rc[:, 5:6]),
                  r=[RCB], w=[OFA])
                for half in range(2):
                    for q in range(4):
                        c = half * 4 + q
                        T(lambda e, c=c, q=q, cg=cg: e.transpose(
                            psT[:, q * 128:(q + 1) * 128],
                            h2b[:, c, cg * 128:(cg + 1) * 128], ident[:, :]),
                          r=[H2B, CONST], w=[PSTB])
                    for q in range(4):
                        c = half * 4 + q
                        C(lambda e, c=c, q=q: e.activation(
                            out=tmb[:, 0, c * 128:(c + 1) * 128],
                            in_=psT[:, q * 128:(q + 1) * 128], func=AF.Copy),
                          r=[PSTB], w=[TMB])
                idmaG(lambda e, g=g: e.indirect_dma_start(
                    out=hbuck_d[:, :],
                    out_offset=bass.IndirectOffsetOnAxis(ap=offu_all[:, g:g + 1],
                                                         axis=0),
                    in_=tmb[:, 0, :], in_offset=None),
                  r=[TMB, OFA], w=[HBUCKD])
                for half in range(2):
                    for q in range(4):
                        c = half * 4 + q
                        T(lambda e, c=c, q=q, cg=cg: e.transpose(
                            psT[:, q * 128:(q + 1) * 128],
                            xb2[:, c, cg * 128:(cg + 1) * 128], ident[:, :]),
                          r=[XB2, CONST], w=[PSTB])
                    for q in range(4):
                        c = half * 4 + q
                        C(lambda e, c=c, q=q: e.activation(
                            out=tmb[:, 1, c * 128:(c + 1) * 128],
                            in_=psT[:, q * 128:(q + 1) * 128], func=AF.Copy),
                          r=[PSTB], w=[TMB])
                dmaG(xmid_d[g * 128:(g + 1) * 128, :], tmb[:, 1, :],
                     r=[TMB], w=[XMIDD])

        if DEBUG["on"]:
            dmaG(dbg["d_rout"][:, 0:32], slot_all[:, :], r=[SLA], w=[DBGB])
            dmaG(dbg["d_rout"][:, 32:64], eidx_all[:, :], r=[EIA], w=[DBGB])
            dmaG(dbg["d_rout"][:, 64:96], gate_all[:, :], r=[GAA], w=[DBGB])
        CIN, COUT, CALL, BROWB, BASEB = bf("CIN", "COUT", "CALL", "BROWB", "BASEB")
        dmaG(cntin_d[0:1, :], cntrow[0:1, :], r=[CNT], w=[CIN])
        G(lambda e: e.collective_compute(
            "AllGather", AL.bypass, ins=[cntin_d[:, :]], outs=[cntout_d[:, :, :]],
            replica_groups=[[0, 1, 2, 3, 4, 5, 6, 7]]),
          r=[CIN], w=[COUT])
        dmaS(cntall[:, :], cntout_d[:, 0, :], r=[COUT], w=[CALL])
        T(lambda e: e.matmul(psR[0:1, 32:40], prevmask[:, :], cntall[:, :],
                             start=True, stop=True),
          r=[CONST, CALL], w=[PSRB])
        V(lambda e: e.tensor_copy(out=brow[0:1, :], in_=psR[0:1, 32:40]),
          r=[PSRB], w=[BROWB])
        T(lambda e: e.matmul(psR[:, 40:48], ones1x[:, :], brow[0:1, :],
                             start=True, stop=True),
          r=[CONST, BROWB], w=[PSRB])
        V(lambda e: e.tensor_copy(out=basebc[:, :], in_=psR[:, 40:48]),
          r=[PSRB], w=[BASEB])
        for g in range(32):
            V(lambda e, g=g: e.tensor_scalar(out=rt[:, 0, :], in0=iota8bc[:, :],
                                             scalar1=eidx_all[:, g:g + 1],
                                             scalar2=None, op0=AL.is_equal),
              r=[CONST, EIA], w=[RTB])
            V(lambda e: e.tensor_tensor(out=rt[:, 0, :], in0=rt[:, 0, :],
                                        in1=basebc[:, :], op=AL.mult),
              r=[BASEB, RTB], w=[RTB])
            V(lambda e: e.tensor_reduce(out=rc[:, 0:1], in_=rt[:, 0, :], axis=AX.X,
                                        op=AL.add),
              r=[RTB], w=[RCB])
            V(lambda e, g=g: e.tensor_tensor(out=rc[:, 0:1], in0=rc[:, 0:1],
                                             in1=slot_all[:, g:g + 1], op=AL.add),
              r=[SLA, RCB], w=[RCB])
            C(lambda e: e.activation(out=rc[:, 0:1], in_=rc[:, 0:1],
                                     func=AF.Sign, scale=ccol[:, 2:3],
                                     bias=ccol[:, 3:4]),
              r=[RCB, CONST], w=[RCB])
            C(lambda e: e.activation(out=rc[:, 0:1], in_=rc[:, 0:1], func=AF.Relu),
              r=[RCB], w=[RCB])
            V(lambda e, g=g: e.tensor_tensor(out=mult_all[:, g:g + 1], in0=rc[:, 0:1],
                                             in1=gate_all[:, g:g + 1], op=AL.mult),
              r=[RCB, GAA], w=[MUA])

        if DEBUG["on"]:
            dmaG(dbg["d_rout"][:, 96:128], mult_all[:, :], r=[MUA], w=[DBGB])
            dmaG(dbg["d_rout"][:, 128:136], gac[:, :], r=[GAC], w=[DBGB])
            dmaG(dbg["d_rout"][:, 136:144], gbc[:, :], r=[GBC], w=[DBGB])
            dmaG(dbg["d_rout"][:, 144:152], capc[:, :], r=[CAPC], w=[DBGB])
            dmaG(dbg["d_rout"][:, 152:160], cbpc[:, :], r=[CBPC], w=[DBGB])
        p.barrier()
        HBTM = [bf("HBTM0"), bf("HBTM1")]
        HBFM, HIDM, YTMB, YFMT = bf("HBFM", "HIDM", "YTMB", "YFMT")
        W1S3 = [bf("W1S30"), bf("W1S31")]
        W2S3 = [bf("W2S30"), bf("W2S31")]
        YBUCKD = bf("YBUCKD")

        for ex in range(8):
            for gg in range(5):
                sl = gg % 2
                dmaS(hbtm[:, sl, :],
                     hbuck_d[ex * CAP + gg * 128:ex * CAP + (gg + 1) * 128, :],
                     r=[HBUCKD], w=[HBTM[sl]])
                for half in range(2):
                    for q in range(4):
                        c = half * 4 + q
                        T(lambda e, c=c, q=q, sl=sl: e.transpose(
                            psT[:, q * 128:(q + 1) * 128],
                            hbtm[:, sl, c * 128:(c + 1) * 128], ident[:, :]),
                          r=[HBTM[sl], CONST], w=[PSTB])
                    for q in range(4):
                        c = half * 4 + q
                        C(lambda e, c=c, q=q, gg=gg: e.activation(
                            out=hbfm[:, c, gg * 128:(gg + 1) * 128],
                            in_=psT[:, q * 128:(q + 1) * 128], func=AF.Copy),
                          r=[PSTB], w=[HBFM])
            dmaS(w1sl3[:, 0, :, :],
                 w1e_d[ex, :, 0:128].rearrange("(c p) f -> p c f", p=128), w=[W1S3[0]])
            for fc in range(32):
                sl = fc % 2
                if fc + 1 < 32:
                    nsl = (fc + 1) % 2
                    dmaS(w1sl3[:, nsl, :, :],
                         w1e_d[ex, :, (fc + 1) * 128:(fc + 2) * 128].rearrange(
                             "(c p) f -> p c f", p=128),
                         w=[W1S3[nsl]])
                for (nb, n0, nn) in ((0, 0, 512), (1, 512, 128)):
                    ps = psA if nb == 0 else psB
                    PSX = PSAB if nb == 0 else PSBB
                    for c in range(8):
                        T(lambda e, sl=sl, c=c, ps=ps, n0=n0, nn=nn: e.matmul(
                            ps[:, 0:nn], w1sl3[:, sl, c, :], hbfm[:, c, n0:n0 + nn],
                            start=(c == 0), stop=(c == 7)),
                          r=[W1S3[sl], HBFM], w=[PSX])
                    C(lambda e, fc=fc, ps=ps, n0=n0, nn=nn: e.activation(
                        out=hidm[:, fc, n0:n0 + nn], in_=ps[:, 0:nn],
                        func=AF.Gelu_apprx_tanh),
                      r=[PSX], w=[HIDM])
            for dc in range(8):
                dmaS(w2sl3[:, 0, :, :],
                     w2e_d[ex, :, dc * 128:(dc + 1) * 128].rearrange(
                         "(c p) d -> p c d", p=128),
                     w=[W2S3[0]])
                for (nb, n0, nn) in ((0, 0, 512), (1, 512, 128)):
                    ps = psA if nb == 0 else psB
                    PSX = PSAB if nb == 0 else PSBB
                    for fc in range(32):
                        T(lambda e, fc=fc, ps=ps, n0=n0, nn=nn: e.matmul(
                            ps[:, 0:nn], w2sl3[:, 0, fc, :], hidm[:, fc, n0:n0 + nn],
                            start=(fc == 0), stop=(fc == 31)),
                          r=[W2S3[0], HIDM], w=[PSX])
                    V(lambda e, ps=ps, nn=nn: e.tensor_copy(out=yfmt[:, 0:nn],
                                                            in_=ps[:, 0:nn]),
                      r=[PSX], w=[YFMT])
                    for cb in range(nn // 128):
                        gg = (n0 + cb * 128) // 128
                        T(lambda e, cb=cb: e.transpose(
                            psT[:, 0:128], yfmt[:, cb * 128:(cb + 1) * 128],
                            ident[:, :]),
                          r=[YFMT, CONST], w=[PSTB])
                        C(lambda e, gg=gg, dc=dc: e.activation(
                            out=ytm[:, gg, dc * 128:(dc + 1) * 128],
                            in_=psT[:, 0:128], func=AF.Copy),
                          r=[PSTB], w=[YTMB])
            for gg in range(5):
                dmaG(ybuck_d[ex * CAP + gg * 128:ex * CAP + (gg + 1) * 128, :],
                     ytm[:, gg, :], r=[YTMB], w=[YBUCKD])

        p.barrier()
        XMTM = [bf("XMTM0"), bf("XMTM1")]
        YGB = [bf("YGB0"), bf("YGB1")]
        OUTB = [bf("OUTB0"), bf("OUTB1")]
        OUTD = bf("OUTD")
        for g in range(32):
            sl = g % 2
            dmaS(xmtm[:, sl, :], xmid_d[g * 128:(g + 1) * 128, :],
                 r=[XMIDD], w=[XMTM[sl]])
            idmaG(lambda e, g=g, sl=sl: e.indirect_dma_start(
                out=ygb[:, sl, :], out_offset=None,
                in_=ybuck_d[:, :],
                in_offset=bass.IndirectOffsetOnAxis(ap=offu_all[:, g:g + 1], axis=0)),
              r=[YBUCKD, OFA], w=[YGB[sl]])
            V(lambda e, g=g, sl=sl: e.tensor_scalar(
                out=ygb[:, sl, :], in0=ygb[:, sl, :], scalar1=mult_all[:, g:g + 1],
                scalar2=None, op0=AL.mult),
              r=[MUA], w=[YGB[sl]])
            V(lambda e, sl=sl: e.tensor_tensor(out=ygb[:, sl, :], in0=ygb[:, sl, :],
                                               in1=gbbc[:, :], op=AL.mult),
              r=[GBBC], w=[YGB[sl]])
            V(lambda e, sl=sl: e.tensor_tensor(out=outb[:, sl, :], in0=ygb[:, sl, :],
                                               in1=xmtm[:, sl, :], op=AL.add),
              r=[YGB[sl], XMTM[sl]], w=[OUTB[sl]])
            dmaG(out_d[g * 128:(g + 1) * 128, :], outb[:, sl, :],
                 r=[OUTB[sl]], w=[OUTD])
        p.raw_wait("G", "dG")
        p.raw_wait("S", "dS")

    with nc.Block() as block:
        @block.tensor
        def _(eng):
            schedule(Ctx("tensor", eng, sems))

        @block.vector
        def _(eng):
            schedule(Ctx("vector", eng, sems))

        @block.scalar
        def _(eng):
            schedule(Ctx("scalar", eng, sems))

        @block.gpsimd
        def _(eng):
            schedule(Ctx("gpsimd", eng, sems))

        @block.sync
        def _(eng):
            schedule(Ctx("sync", eng, sems))

    return nc


def _fm_vec(v, n=8):
    return np.ascontiguousarray(np.asarray(v, np.float32).reshape(n, 128).T)


def kernel(L0_attn_x, L0_attn_W_to, L0_attn_log_decay, L0_attn_frequencies,
           L0_attn_W_from, L0_attn_ln1_g, L0_attn_ln1_b, L0_attn_ln3_g, L0_attn_ln3_b,
           L0_attn_W_ffn1, L0_attn_b_ffn1, L0_attn_W_ffn2, L0_attn_b_ffn2,
           L0_ffn_router_w, L0_ffn_w1, L0_ffn_w2, L0_c, L0_wmod):
    f32 = lambda a: np.ascontiguousarray(np.asarray(a, np.float32))
    x = f32(L0_attn_x)
    wtoT = f32(np.asarray(L0_attn_W_to, np.float32).T)
    wfromT = f32(np.asarray(L0_attn_W_from, np.float32).T)
    w1T = f32(np.asarray(L0_attn_W_ffn1, np.float32).T)
    w2T = f32(np.asarray(L0_attn_W_ffn2, np.float32).T)
    wmod = f32(L0_wmod)
    rw = f32(L0_ffn_router_w)
    w1e = f32(L0_ffn_w1)
    w2e = f32(L0_ffn_w2)
    cmat = f32(L0_c)
    vecs = np.zeros((128, 76), np.float32)
    vecs[:, 0:8] = _fm_vec(L0_attn_ln1_g)
    vecs[:, 8:16] = _fm_vec(L0_attn_ln1_b)
    vecs[:, 16:24] = _fm_vec(L0_attn_ln3_g)
    vecs[:, 24:32] = _fm_vec(L0_attn_ln3_b)
    vecs[:, 32:40] = _fm_vec(L0_attn_b_ffn2)
    vecs[:, 40:42] = _fm_vec(L0_attn_log_decay, 2)
    vecs[:, 42:44] = _fm_vec(L0_attn_frequencies, 2)
    vecs[:, 44:76] = _fm_vec(L0_attn_b_ffn1, 32)
    vecs = np.ascontiguousarray(vecs)

    ident = np.eye(128, dtype=np.float32)
    ones1x = np.ones((1, 128), np.float32)
    onescol = np.ones((128, 1), np.float32)
    u128 = (np.arange(128)[:, None] <= np.arange(128)[None, :]).astype(np.float32)
    iotabc = np.ascontiguousarray(
        np.broadcast_to(np.arange(512, dtype=np.float32)[None, :], (128, 512)))
    iota8bc = np.ascontiguousarray(
        np.broadcast_to(np.arange(8, dtype=np.float32)[None, :], (128, 8)))

    in_maps = []
    for k in range(8):
        b, h = k >> 1, k & 1
        if h == 1:
            xcore = np.ascontiguousarray(x[b].T)
        else:
            xcore = np.zeros((1024, 8192), np.float32)
            xcore[:, 4096:] = x[b, 0:4096].T
        tmaskc = np.zeros((128, 16), np.float32)
        if h == 0:
            tmaskc[:, 8:] = 0.1
        else:
            tmaskc[:, :] = 0.1
        prevmask = np.zeros((8, 1), np.float32)
        prevmask[:k, 0] = 1.0
        m = dict(x_fm=xcore, wtoT=wtoT, wfromT=wfromT, w1T=w1T, w2T=w2T,
                 wmod=wmod, rw=rw, w1e=w1e, w2e=w2e,
                 crow=_fm_vec(cmat[b]), vecs=vecs, ident=ident, ones1x=ones1x,
                 onescol=onescol, u128=u128, iotabc=iotabc, iota8bc=iota8bc,
                 prevmask=prevmask, tmaskc=np.ascontiguousarray(tmaskc),
                 epsc=np.full((1, 1), EPS, np.float32),
                 ccol=np.ascontiguousarray(np.broadcast_to(
                     np.array([0.5, 1.0, -1.0, 4095.5], np.float32)[None, :],
                     (128, 4))))
        in_maps.append(m)

    nc = build_program()
    res = run_bass_kernel_spmd(nc, in_maps, core_ids=list(range(8)))

    full = np.empty((B, S, D), np.float32)
    for k in range(8):
        b, h = k >> 1, k & 1
        o = np.asarray(res.results[k]["out"])
        full[b, h * 4096:(h + 1) * 4096, :] = o
    return full
